# version 03
# baseline (speedup 1.0000x reference)
import numpy as np
import ml_dtypes
import concourse.bass as bass
import concourse.mybir as mybir
from concourse.bass_utils import run_bass_kernel_spmd

F32 = mybir.dt.float32
BF16 = mybir.dt.bfloat16
AF = mybir.ActivationFunctionType
ALU = mybir.AluOpType
AX = mybir.AxisListType
BF = ml_dtypes.bfloat16

D = 1024
NC_ = 8
NB = 32
H = 16
DH = 64
NE = 64
EH = 256
ES = 2
HS = 2048
H1 = 4096
LN_EPS = 1e-5
ATTN_CUT = 99
SOFT_CUT = 99

ENGINES = ("sync", "gpsimd", "vector", "scalar", "tensor")


class Sem:
    __slots__ = ("h", "name", "n")

    def __init__(self, nc, name):
        self.h = nc.alloc_semaphore(name)
        self.name = name
        self.n = 0


class Buf:
    __slots__ = ("name", "w", "r")

    def __init__(self, name=""):
        self.name = name
        self.w = None
        self.r = []


class P:
    def __init__(self, nc):
        self.nc = nc
        self.streams = {e: [] for e in ENGINES}
        self.waited = {e: {} for e in ENGINES}
        self.sems = {}
        self.dma_sems = set()
        self.qcls = {}
        self.es = {e: self.sem("es_" + e) for e in ENGINES}

    def sem(self, name):
        s = Sem(self.nc, name)
        self.sems[name] = s
        return s

    def q(self, name, eng):
        cls = "sw" if eng == "gpsimd" else "hw"
        if name not in self.sems:
            s = self.sem(name)
            self.dma_sems.add(name)
            self.qcls[name] = cls
        assert self.qcls[name] == cls, f"sem {name} used by both HWDGE and SWDGE"
        return self.sems[name]

    def on(self, eng, fn):
        self.streams[eng].append(fn)

    def wait(self, eng, sem, val):
        if sem is None or val <= 0:
            return
        if self.waited[eng].get(sem.name, 0) >= val:
            return
        if sem.name in self.dma_sems and val < sem.n:
            val = sem.n
            if self.waited[eng].get(sem.name, 0) >= val:
                return
        self.waited[eng][sem.name] = val
        h = sem.h
        self.on(eng, lambda e: e.wait_ge(h, val))

    def _pre(self, eng, reads, writes):
        for b in reads:
            if b.w:
                self.wait(eng, *b.w)
        for b in writes:
            if b.w:
                self.wait(eng, *b.w)
            for (s, v) in b.r:
                self.wait(eng, s, v)

    def _post(self, sem, tgt, reads, writes):
        for b in writes:
            b.w = (sem, tgt)
            b.r = []
        for b in reads:
            b.r.append((sem, tgt))

    def op(self, eng, fn, reads=(), writes=()):
        self._pre(eng, reads, writes)
        sem = self.es[eng]
        sem.n += 1
        h = sem.h
        self.on(eng, lambda e: fn(e).then_inc(h, 1))
        self._post(sem, sem.n, reads, writes)
        return sem.n

    def dma(self, eng, sem, fn, reads=(), writes=()):
        for b in reads:
            if b.w and b.w[0] is not sem:
                self.wait(eng, *b.w)
        for b in writes:
            if b.w and b.w[0] is not sem:
                self.wait(eng, *b.w)
            for (s, v) in b.r:
                self.wait(eng, s, v)
        sem.n += 16
        h = sem.h
        self.on(eng, lambda e: fn(e).then_inc(h, 16))
        self._post(sem, sem.n, reads, writes)
        return sem.n

    def mm_group(self, fns, reads=(), writes=(), inc_each=False):
        self._pre("tensor", reads, writes)
        sem = self.es["tensor"]
        h = sem.h
        if inc_each:
            for fn in fns:
                sem.n += 1
                self.on("tensor", lambda e, fn=fn: fn(e).then_inc(h, 1))
        else:
            for fn in fns[:-1]:
                self.on("tensor", fn)
            sem.n += 1
            fnl = fns[-1]
            self.on("tensor", lambda e: fnl(e).then_inc(h, 1))
        self._post(sem, sem.n, reads, writes)
        return sem.n

    def barrier(self):
        snap = [(s, s.n) for s in self.sems.values() if s.n > 0]
        for e in ENGINES:
            for s, v in snap:
                self.wait(e, s, v)

    def build(self):
        nc = self.nc
        with nc.Block() as block:
            for eng in ENGINES:
                fns = self.streams[eng]

                def body(e, fns=fns):
                    for fn in fns:
                        fn(e)

                getattr(block, eng)(body)




def ceil_div(a, b):
    return -(-a // b)


class Consts:

    def __init__(self, p):
        nc = p.nc
        self.identf = nc.inline_tensor(np.eye(128, dtype=np.float32), "c_identf").ap()
        self.identb = nc.inline_tensor(np.eye(128, dtype=BF), "c_identb").ap()
        ones_col_b = nc.inline_tensor(np.ones((128, 1), dtype=BF), "c_ones_col_b").ap()
        ones_row_b = nc.inline_tensor(np.ones((1, 128), dtype=BF), "c_ones_row_b").ap()
        ones_row_f = nc.inline_tensor(np.ones((1, 128), dtype=np.float32), "c_ones_row_f").ap()
        causal = np.where(np.tril(np.ones((NB, NB))), 0.0, -3e38).astype(np.float32)
        self.causal = nc.inline_tensor(causal, "c_causal").ap()
        jrev = (np.arange(255, -1, -1, dtype=np.float32))[None, :]
        self.jrev = nc.inline_tensor(jrev, "c_jrev").ap()
        jfwd = (np.arange(1, 257, dtype=np.float32))[None, :]
        self.jfwd = nc.inline_tensor(jfwd, "c_jfwd").ap()

        self.idf = nc.alloc_sbuf_tensor("k_idf", [128, 128], F32).ap()
        self.idb = nc.alloc_sbuf_tensor("k_idb", [128, 128], BF16).ap()
        self.onecb = nc.alloc_sbuf_tensor("k_onecb", [128, 1], BF16).ap()
        self.onerb = nc.alloc_sbuf_tensor("k_onerb", [1, 128], BF16).ap()
        self.onerf = nc.alloc_sbuf_tensor("k_onerf", [1, 128], F32).ap()
        self.caus = nc.alloc_sbuf_tensor("k_caus", [NB, NB], F32).ap()

        self.qc = p.q("q_const", "sync")
        b = Buf("consts")
        p.dma("sync", self.qc, lambda e: e.dma_start(out=self.idf, in_=self.identf), writes=(b,))
        p.dma("sync", self.qc, lambda e: e.dma_start(out=self.idb, in_=self.identb), writes=(b,))
        p.dma("sync", self.qc, lambda e: e.dma_start(out=self.onecb, in_=ones_col_b), writes=(b,))
        p.dma("sync", self.qc, lambda e: e.dma_start(out=self.onerb, in_=ones_row_b), writes=(b,))
        p.dma("sync", self.qc, lambda e: e.dma_start(out=self.onerf, in_=ones_row_f), writes=(b,))
        p.dma("sync", self.qc, lambda e: e.dma_start(out=self.caus, in_=self.causal), writes=(b,))
        self.buf = b


def load_row(p, q, dram_ap, row_sb, buf):
    p.dma("sync", q, lambda e: e.dma_start(out=row_sb, in_=dram_ap), writes=(buf,))


def row_to_cols(p, C, ps_bank, ps_buf, row_sb, row_buf, col_sb, col_buf, n, scale=None):
    fns = [TRG(ps_bank[:, c:c + 1], row_sb[0:1, c * 128:(c + 1) * 128],
               C.idf[0:1, 0:1], c == 0, c == n - 1) for c in range(n)]
    p.mm_group(fns, reads=(row_buf, C.buf), writes=(ps_buf,), inc_each=True)
    if scale is None:
        p.op("vector", lambda e: e.tensor_copy(col_sb[:, 0:n], ps_bank[:, 0:n]),
             reads=(ps_buf,), writes=(col_buf,))
    else:
        p.op("vector", lambda e: e.tensor_scalar(col_sb[:, 0:n], ps_bank[:, 0:n], scale, None, op0=ALU.mult),
             reads=(ps_buf,), writes=(col_buf,))


def mat_to_cols(p, C, ps_bank, ps_buf, mat_sb, mat_buf, col_sb, col_buf, rows, n):
    fns = [TRG(ps_bank[:, c * rows:(c + 1) * rows], mat_sb[0:rows, c * 128:(c + 1) * 128],
               C.idf[0:rows, 0:rows], c == 0, c == n - 1) for c in range(n)]
    p.mm_group(fns, reads=(mat_buf, C.buf), writes=(ps_buf,), inc_each=True)
    p.op("vector", lambda e: e.tensor_copy(col_sb[:, 0:n * rows], ps_bank[:, 0:n * rows]),
         reads=(ps_buf,), writes=(col_buf,))


class Psum:

    def __init__(self, p):
        nc = p.nc
        self.f = [nc.alloc_psum_tensor(f"pb{i}", [128, 512], F32).ap() for i in range(7)]
        self.fb = [Buf(f"pb{i}") for i in range(7)]
        self.b = nc.alloc_psum_tensor("pbb", [128, 1024], BF16).ap()
        self.bb = Buf("pbb")




def emit_ln(p, C, ps, xf, xf_buf, xln3, xln_buf, scr, TS, gam_col=None, bet_col=None,
            colbuf=None):
    nc = p.nc
    st_bank, st_buf = ps.f[5], ps.fb[5]
    bc_bank, bc_buf = ps.f[6], ps.fb[6]
    xbb, xbb_buf = scr["xbb"], scr["xbb_buf"]
    xsq, xsq_buf = scr["xsq"], scr["xsq_buf"]
    rows, rows_buf = scr["rows"], scr["rows_buf"]
    assert TS <= 256 or TS == 512

    p.op("scalar", lambda e: e.activation(xbb, xf, AF.Identity),
         reads=(xf_buf,), writes=(xbb_buf,))
    p.op("scalar", lambda e: e.activation(xsq, xf, AF.Square),
         reads=(xf_buf,), writes=(xsq_buf,))
    assert TS <= 256
    fns = [lambda e, c=c: e.matmul(st_bank[0:1, 0:TS], C.onecb, xbb[:, c, :],
                                   start=(c == 0), stop=(c == 7)) for c in range(8)]
    p.mm_group(fns, reads=(xbb_buf, C.buf), writes=(st_buf,))
    fns = [lambda e, c=c: e.matmul(st_bank[0:1, TS:2 * TS], C.onecb, xsq[:, c, :],
                                   start=(c == 0), stop=(c == 7)) for c in range(8)]
    p.mm_group(fns, reads=(xsq_buf, C.buf), writes=(st_buf,))
    mu = rows[0:1, 0:TS]
    e2 = rows[0:1, TS:2 * TS]
    rs = rows[0:1, 2 * TS:3 * TS]
    p.op("vector", lambda e: e.tensor_scalar(mu, st_bank[0:1, 0:TS], 1.0 / D, None, op0=ALU.mult),
         reads=(st_buf,), writes=(rows_buf,))
    p.op("vector", lambda e: e.tensor_scalar(e2, st_bank[0:1, TS:2 * TS], 1.0 / D, None, op0=ALU.mult),
         reads=(st_buf,), writes=(rows_buf,))
    p.op("vector", lambda e: e.tensor_tensor(rs, mu, mu, op=ALU.mult),
         reads=(rows_buf,), writes=(rows_buf,))
    p.op("vector", lambda e: e.tensor_sub(e2, e2, rs), reads=(rows_buf,), writes=(rows_buf,))
    p.op("scalar", lambda e: e.activation(rs, e2, AF.Sqrt, bias=scr["epsrow"], scale=1.0),
         reads=(rows_buf, scr["eps_buf"]), writes=(rows_buf,))
    p.op("vector", lambda e: e.reciprocal(rs, rs), reads=(rows_buf,), writes=(rows_buf,))
    p.mm_group([lambda e: e.matmul(bc_bank[:, 0:TS], C.onerf, mu, start=True, stop=True)],
               reads=(rows_buf, C.buf), writes=(bc_buf,))
    p.mm_group([lambda e: e.matmul(bc_bank[:, 256:256 + TS], C.onerf, rs, start=True, stop=True)],
               reads=(rows_buf, C.buf), writes=(bc_buf,))
    for c in range(8):
        p.op("vector", lambda e, c=c: e.tensor_sub(scr["t1"][:, c, :], xf[:, c, :], bc_bank[:, 0:TS]),
             reads=(xf_buf, bc_buf), writes=(scr["t1_buf"],))
    for c in range(8):
        if gam_col is None:
            p.op("vector", lambda e, c=c: e.tensor_mul(xln3[:, c, :], scr["t1"][:, c, :], bc_bank[:, 256:256 + TS]),
                 reads=(scr["t1_buf"], bc_buf), writes=(xln_buf,))
        else:
            p.op("vector", lambda e, c=c: e.tensor_mul(scr["t1"][:, c, :], scr["t1"][:, c, :], bc_bank[:, 256:256 + TS]),
                 reads=(scr["t1_buf"], bc_buf), writes=(scr["t1_buf"],))
            p.op("vector", lambda e, c=c: e.tensor_scalar(xln3[:, c, :], scr["t1"][:, c, :],
                                                          gam_col[:, c:c + 1], bet_col[:, c:c + 1],
                                                          op0=ALU.mult, op1=ALU.add),
                 reads=(scr["t1_buf"], colbuf), writes=(xln_buf,))


def make_ln_scratch(p, ctx, TS, tag):
    nc = p.nc
    xbb = ctx.enter_context(nc.sbuf_tensor(f"{tag}_xbb", [128, 8, TS], BF16)).ap()
    xsq = ctx.enter_context(nc.sbuf_tensor(f"{tag}_xsq", [128, 8, TS], BF16)).ap()
    t1 = ctx.enter_context(nc.sbuf_tensor(f"{tag}_t1", [128, 8, TS], F32)).ap()
    rows = ctx.enter_context(nc.sbuf_tensor(f"{tag}_rows", [1, 3 * TS], F32)).ap()
    epsrow = ctx.enter_context(nc.sbuf_tensor(f"{tag}_eps", [1, 1], F32)).ap()
    eps_buf = Buf("eps")
    p.op("vector", lambda e: e.memset(epsrow, LN_EPS), writes=(eps_buf,))
    return {
        "xbb": xbb, "xbb_buf": Buf("xbb"), "xsq": xsq, "xsq_buf": Buf("xsq"),
        "t1": t1, "t1_buf": Buf("t1"), "rows": rows, "rows_buf": Buf("rows"),
        "epsrow": epsrow, "eps_buf": eps_buf,
    }


def MM(o, l, r, st, sp, tp=None, skip=False):
    if tp is None and not skip:
        return lambda e: e.matmul(o, l, r, start=st, stop=sp)
    return lambda e: e.matmul(o, l, r, start=st, stop=sp, tile_position=tp,
                              skip_group_check=skip)


def TR(o, i_, ident):
    return lambda e: e.transpose(o, i_, ident)


def TRG(o, i_, ident, st, sp):
    return lambda e: e.matmul(o, i_, ident, start=st, stop=sp, is_transpose=True)


def DMA(o, i_):
    return lambda e: e.dma_start(out=o, in_=i_)




def emit_moe(p, C, ps, ctx, xin, xout, W, NT, uid=""):
    nc = p.nc
    TB = min(1024, NT)
    NTILE = NT // TB
    TS = 256
    NSUB = TB // TS
    G = 4
    NG = NE // G

    sbuf = lambda n, s, d: ctx.enter_context(nc.sbuf_tensor(n + uid, s, d)).ap()
    xf = [sbuf(f"moe_xf{i}", [128, 8, TS], F32) for i in range(2)]
    xf_b = [Buf(f"xf{i}") for i in range(2)]
    xln = sbuf("moe_xln", [128, 8, TB], BF16)
    xln4 = xln.rearrange("p c (n t) -> p c n t", t=TS)
    xln_b = Buf("xln")
    routed = sbuf("moe_routed", [128, 8, TB], F32)
    rv = routed.rearrange("p c (n t) -> p c n t", t=TS)
    routed_b = [Buf(f"routed{j}") for j in range(4)]
    cwT = sbuf("moe_cwT", [64, TB], BF16)
    cwT_b = Buf("cwT")
    cwrow = [sbuf(f"moe_cwrow{i}", [1, TB], BF16) for i in range(2)]
    cwrow_b = [Buf(f"cwrow{i}") for i in range(2)]
    cwb = [sbuf(f"moe_cwb{i}", [128, TB], BF16) for i in range(G)]
    cwb_b = [Buf(f"cwb{i}") for i in range(G)]
    w1g = [sbuf(f"moe_w1g{i}", [128, G, 8, 256], BF16) for i in range(2)]
    w2g = [sbuf(f"moe_w2g{i}", [128, G, 2, 1024], BF16) for i in range(2)]
    w1g_b = [Buf(f"w1g{i}") for i in range(2)]
    w2g_b = [Buf(f"w2g{i}") for i in range(2)]
    shw1 = [w1g[i][:, :, :, 0:128] for i in range(2)]
    shw2 = [w2g[i][:, :, 0, :] for i in range(2)]
    hsb = [sbuf(f"moe_hsb{i}", [128, 2, TS], BF16) for i in range(2)]
    hsb_b = [Buf(f"hsb{i}") for i in range(2)]
    rwsb = sbuf("moe_rwsb", [128, 8, 64], BF16)
    rwsb_b = Buf("rwsb")
    b2sb = sbuf("moe_b2sb", [64, 1024], BF16)
    b2sb_b = Buf("b2sb")
    lgt = sbuf("moe_lgt", [64, TS], F32)
    lgt_b = Buf("lgt")
    rtok = sbuf("moe_rtok", [128, 64 * 4 + 8], F32)
    rtok_b = Buf("rtok")
    r1 = sbuf("moe_r1", [128, 4], F32)
    r1_b = Buf("r1")
    cwtok = sbuf("moe_cwtok", [128, 64], BF16)
    cwtok_b = Buf("cwtok")
    B1c = sbuf("moe_B1c", [128, 2 * 64], F32)
    B1c_b = Buf("B1c")
    SB1c = sbuf("moe_SB1c", [128, 32], F32)
    SB1c_b = Buf("SB1c")
    SB2h = sbuf("moe_SB2h", [128, 8], F32)
    SB2h_b = Buf("SB2h")
    rbc = sbuf("moe_rbc", [64, 1], F32)
    rbc_b = Buf("rbc")
    rowscr = sbuf("moe_rowscr", [1, 2048], F32)
    rowscr_b = Buf("rowscr")
    b1sb = sbuf("moe_b1sb", [64, 256], F32)
    b1sb_b = Buf("b1sb")
    sb1sb = sbuf("moe_sb1sb", [2, 2048], F32)
    sb1sb_b = Buf("sb1sb")
    sb2b = sbuf("moe_sb2b", [1, 1024], F32)
    sb2b_b = Buf("sb2b")
    scr = make_ln_scratch(p, ctx, TS, "moe" + uid)

    qx = [p.q(f"qx{i}", "sync") for i in range(2)]
    qst = [p.q(f"qst{i}", "sync") for i in range(2)]
    qw1 = [p.q(f"qw1_{i}", "gpsimd") for i in range(2)]
    qw2 = [p.q(f"qw2_{i}", "gpsimd") for i in range(2)]
    qsh1 = [p.q(f"qsh1_{i}", "gpsimd") for i in range(2)]
    qsh2 = [p.q(f"qsh2_{i}", "gpsimd") for i in range(2)]
    qr = [p.q(f"qr{i}", "scalar") for i in range(2)]
    qpg = [p.q(f"qpg{i}", "gpsimd") for i in range(2)]
    qps = [p.q(f"qps{i}", "sync") for i in range(4)]

    xin3 = xin.rearrange("(c p) t -> p c t", p=128)
    xout3 = xout.rearrange("(c p) t -> p c t", p=128)

    p.dma("gpsimd", qpg[0], DMA(rwsb, W["rw"].rearrange("(c p) n -> p c n", p=128)), writes=(rwsb_b,))
    p.dma("gpsimd", qpg[1], DMA(b2sb, W["b2"]), writes=(b2sb_b,))
    p.dma("sync", qps[0], DMA(b1sb, W["b1"]), writes=(b1sb_b,))
    mat_to_cols(p, C, ps.f[5], ps.fb[5], b1sb, b1sb_b, B1c, B1c_b, 64, 2)
    p.dma("sync", qps[1], DMA(sb1sb, W["sb1"]), writes=(sb1sb_b,))
    mat_to_cols(p, C, ps.f[5], ps.fb[5], sb1sb, sb1sb_b, SB1c, SB1c_b, 2, 16)
    p.dma("sync", qps[2], DMA(rowscr[0:1, 0:1024], W["sb2"][0:1, :]), writes=(rowscr_b,))
    p.dma("sync", qps[3], DMA(sb2b, W["sb2"][1:2, :]), writes=(sb2b_b,))
    p.op("vector", lambda e: e.tensor_add(rowscr[0:1, 0:1024], rowscr[0:1, 0:1024], sb2b),
         reads=(sb2b_b,), writes=(rowscr_b,))
    row_to_cols(p, C, ps.f[5], ps.fb[5], rowscr[0:1, 0:1024], rowscr_b, SB2h, SB2h_b, 8, scale=0.5)
    p.dma("sync", qps[2], DMA(rowscr[0:1, 1024:1088], W["rb"]), writes=(rowscr_b,))
    p.mm_group([TR(ps.f[5][0:64, 0:1], rowscr[0:1, 1024:1088], C.idf[0:1, 0:1])],
               reads=(rowscr_b, C.buf), writes=(ps.fb[5],))
    p.op("vector", lambda e: e.tensor_copy(rbc, ps.f[5][0:64, 0:1]), reads=(ps.fb[5],), writes=(rbc_b,))

    def load_group(g, slot):
        for i in range(G):
            p.dma("gpsimd", qw1[slot], DMA(w1g[slot][:, i, :, :],
                  W["w1"][g * G + i].rearrange("(kc p) h -> p kc h", p=128)), writes=(w1g_b[slot],))
            p.dma("gpsimd", qw2[slot], DMA(w2g[slot][:, i, :, :],
                  W["w2"][g * G + i].rearrange("(kh p) d -> p kh d", p=128)), writes=(w2g_b[slot],))

    def load_shared(sg, slot):
        s, hc0 = sg // 4, 4 * (sg % 4)
        for i in range(4):
            hc = hc0 + i
            p.dma("gpsimd", qsh1[slot], DMA(shw1[slot][:, i, :, :],
                  W["sw1"][s, :, hc * 128:(hc + 1) * 128].rearrange("(kc p) h -> p kc h", p=128)),
                  writes=(w1g_b[slot],))
            p.dma("gpsimd", qsh2[slot], DMA(shw2[slot][:, i, :],
                  W["sw2"][s, hc * 128:(hc + 1) * 128, :]), writes=(w2g_b[slot],))

    def build_cwb(i, g):
        ei = g * G + i
        slot = i % 2
        p.dma("scalar", qr[slot], DMA(cwrow[slot], cwT[ei:ei + 1, :]),
              reads=(cwT_b,), writes=(cwrow_b[slot],))
        for j in range(TB // 512):
            p.mm_group([MM(ps.f[6][:, 0:512], C.onerb, cwrow[slot][0:1, j * 512:(j + 1) * 512],
                           True, True)], reads=(cwrow_b[slot], C.buf), writes=(ps.fb[6],))
            dst = cwb[i][:, j * 512:(j + 1) * 512]
            src = ps.f[6][:, 0:512]
            if j % 2 == 0:
                p.op("scalar", lambda e, dst=dst, src=src: e.activation(dst, src, AF.Identity),
                     reads=(ps.fb[6],), writes=(cwb_b[i],))
            else:
                p.op("vector", lambda e, dst=dst, src=src: e.tensor_copy(dst, src),
                     reads=(ps.fb[6],), writes=(cwb_b[i],))

    load_group(0, 0)
    for ti in range(NTILE):
        t0 = ti * TB
        for tsi in range(NSUB):
            tt = t0 + tsi * TS
            sl = tsi % 2
            p.dma("sync", qx[sl], DMA(xf[sl], xin3[:, :, tt:tt + TS]), writes=(xf_b[sl],))
            xlnv = xln4[:, :, tsi, :]
            emit_ln(p, C, ps, xf[sl], xf_b[sl], xlnv, xln_b, scr, TS)
            fns = [MM(ps.f[4][0:64, 0:TS], rwsb[:, c, :], xlnv[:, c, :], c == 0, c == 7)
                   for c in range(8)]
            p.mm_group(fns, reads=(xln_b, rwsb_b), writes=(ps.fb[4],))
            p.op("scalar", lambda e: e.activation(lgt, ps.f[4][0:64, 0:TS], AF.Identity, bias=rbc),
                 reads=(ps.fb[4], rbc_b), writes=(lgt_b,))
            for gt in range(TS // 128):
                p.mm_group([TR(ps.f[4][:, 256:320], lgt[0:64, gt * 128:(gt + 1) * 128],
                               C.idf[0:64, 0:64])], reads=(lgt_b, C.buf), writes=(ps.fb[4],))
                p.op("vector", lambda e: e.tensor_copy(rtok[:, 0:64], ps.f[4][:, 256:320]),
                     reads=(ps.fb[4],), writes=(rtok_b,))
                stok, smask, cwraw, mx8 = (rtok[:, 64:128], rtok[:, 128:192],
                                           rtok[:, 192:256], rtok[:, 256:264])
                p.op("vector", lambda e: e.reduce_max(r1[:, 0:1], rtok[:, 0:64], axis=AX.X, negate=True),
                     reads=(rtok_b,), writes=(r1_b,))
                p.op("scalar", lambda e, s=stok: e.activation(s, rtok[:, 0:64], AF.Exp, bias=r1[:, 0:1]),
                     reads=(rtok_b, r1_b), writes=(rtok_b,))
                p.op("vector", lambda e, s=stok, m=mx8: e.max(out=m, in_=s),
                     reads=(rtok_b,), writes=(rtok_b,))
                p.op("vector", lambda e, s=stok, m=mx8, o=smask: e.match_replace(
                    out=o, in_to_replace=m, in_values=s, imm_value=0.0),
                    reads=(rtok_b,), writes=(rtok_b,))
                p.op("vector", lambda e, s=stok, o=smask, c=cwraw: e.tensor_sub(c, s, o),
                     reads=(rtok_b,), writes=(rtok_b,))
                p.op("vector", lambda e, c=cwraw: e.reduce_sum(r1[:, 1:2], c, axis=AX.X),
                     reads=(rtok_b,), writes=(r1_b,))
                p.op("vector", lambda e: e.reciprocal(r1[:, 2:3], r1[:, 1:2]),
                     reads=(r1_b,), writes=(r1_b,))
                p.op("vector", lambda e, c=cwraw: e.tensor_scalar(cwtok, c, r1[:, 2:3], None, op0=ALU.mult),
                     reads=(rtok_b, r1_b), writes=(cwtok_b,))
                p.mm_group([TR(ps.b[0:64, 0:128], cwtok, C.idb)],
                           reads=(cwtok_b, C.buf), writes=(ps.bb,))
                col = tsi * TS + gt * 128
                dst = cwT[0:64, col:col + 128]
                p.op("vector", lambda e, dst=dst: e.tensor_copy(dst, ps.b[0:64, 0:128]),
                     reads=(ps.bb,), writes=(cwT_b,))
        for g in range(NG):
            slot = g % 2
            if g + 1 < NG:
                load_group(g + 1, (g + 1) % 2)
            for i in range(G):
                build_cwb(i, g)
            for tsi in range(NSUB):
                tt = tsi * TS

                def mm1(i, hp, slot=slot, tsi=tsi):
                    lfns = []
                    for hc in range(2):
                        for kc in range(8):
                            lfns.append(MM(ps.f[hp][:, hc * TS:(hc + 1) * TS],
                                           w1g[slot][:, i, kc, hc * 128:(hc + 1) * 128],
                                           xln4[:, kc, tsi, :], kc == 0, kc == 7))
                    p.mm_group(lfns, reads=(w1g_b[slot], xln_b), writes=(ps.fb[hp],))

                def silu_cw(i, hp, g=g, tt=tt):
                    hh = hsb[i % 2]
                    for hc in range(2):
                        p.op("scalar", lambda e, o=hh[:, hc, :], s=ps.f[hp][:, hc * TS:(hc + 1) * TS],
                             b=B1c[:, hc * 64 + g * G + i: hc * 64 + g * G + i + 1]:
                             e.activation(o, s, AF.Silu, bias=b),
                             reads=(ps.fb[hp], B1c_b), writes=(hsb_b[i % 2],))
                    cv = cwb[i][:, tt:tt + TS].unsqueeze(1).to_broadcast([128, 2, TS])
                    p.op("vector", lambda e, hv=hh, cv=cv: e.tensor_mul(hv, hv, cv),
                         reads=(hsb_b[i % 2], cwb_b[i]), writes=(hsb_b[i % 2],))

                def mm2(i, first, last, g=g, slot=slot, tsi=tsi):
                    lfns = []
                    for dm in range(8):
                        bank, colo = dm // 2, (dm % 2) * TS
                        for kh in range(2):
                            lfns.append(MM(ps.f[bank][:, colo:colo + TS],
                                           w2g[slot][:, i, kh, dm * 128:(dm + 1) * 128],
                                           hsb[i % 2][:, kh, :],
                                           first and kh == 0 and dm % 2 == 0,
                                           last and kh == 1 and dm % 2 == 1))
                    if g == 0 and first:
                        for dm in range(8):
                            bank, colo = dm // 2, (dm % 2) * TS
                            lfns.append(MM(ps.f[bank][:, colo:colo + TS],
                                           b2sb[:, dm * 128:(dm + 1) * 128],
                                           cwT[:, tsi * TS:(tsi + 1) * TS], False, False))
                    p.mm_group(lfns, reads=(hsb_b[i % 2], w2g_b[slot], b2sb_b, cwT_b),
                               writes=tuple(ps.fb[b] for b in range(4)))

                mm1(0, 4)
                for i in range(G):
                    if i + 1 < G:
                        mm1(i + 1, 4 + ((i + 1) % 2))
                    silu_cw(i, 4 + (i % 2))
                    mm2(i, i == 0, i == G - 1)
                for j in range(4):
                    for half in range(2):
                        dst = rv[:, 2 * j + half, tsi, :]
                        src = ps.f[j][:, half * TS:(half + 1) * TS]
                        if g == 0:
                            p.op("vector", lambda e, dst=dst, src=src: e.tensor_copy(dst, src),
                                 reads=(ps.fb[j],), writes=(routed_b[j],))
                        else:
                            p.op("vector", lambda e, dst=dst, src=src: e.tensor_add(dst, dst, src),
                                 reads=(ps.fb[j],), writes=(routed_b[j],))
        load_shared(0, 0)
        for sg in range(8):
            slot = sg % 2
            s_, hc0 = sg // 4, 4 * (sg % 4)
            if sg + 1 < 8:
                load_shared(sg + 1, (sg + 1) % 2)
            for tsi in range(NSUB):
                def smm1(i, hp, slot=slot, tsi=tsi):
                    lfns = [MM(ps.f[hp][:, 0:TS], shw1[slot][:, i, kc, :],
                               xln4[:, kc, tsi, :], kc == 0, kc == 7) for kc in range(8)]
                    p.mm_group(lfns, reads=(w1g_b[slot], xln_b), writes=(ps.fb[hp],))

                def ssilu(i, hp, s_=s_, hc0=hc0):
                    bcol = SB1c[:, (hc0 + i) * 2 + s_: (hc0 + i) * 2 + s_ + 1]
                    p.op("scalar", lambda e, o=hsb[i % 2][:, 0, :], s=ps.f[hp][:, 0:TS], b=bcol:
                         e.activation(o, s, AF.Silu, bias=b),
                         reads=(ps.fb[hp], SB1c_b), writes=(hsb_b[i % 2],))

                def smm2(i, first, last, slot=slot):
                    lfns = []
                    for dm in range(8):
                        bank, colo = dm // 2, (dm % 2) * TS
                        lfns.append(MM(ps.f[bank][:, colo:colo + TS],
                                       shw2[slot][:, i, dm * 128:(dm + 1) * 128],
                                       hsb[i % 2][:, 0, :],
                                       first and dm % 2 == 0,
                                       last and dm % 2 == 1))
                    p.mm_group(lfns, reads=(hsb_b[i % 2], w2g_b[slot]),
                               writes=tuple(ps.fb[b] for b in range(4)))

                smm1(0, 4)
                for i in range(4):
                    if i + 1 < 4:
                        smm1(i + 1, 4 + ((i + 1) % 2))
                    ssilu(i, 4 + (i % 2))
                    smm2(i, i == 0, i == 3)
                for j in range(4):
                    for half in range(2):
                        dst = rv[:, 2 * j + half, tsi, :]
                        src = ps.f[j][:, half * TS:(half + 1) * TS]
                        p.op("vector", lambda e, dst=dst, src=src: e.tensor_add(dst, dst, src),
                             reads=(ps.fb[j],), writes=(routed_b[j],))
        if ti + 1 < NTILE:
            load_group(0, 0)
        for tsi in range(NSUB):
            tt = t0 + tsi * TS
            sl = tsi % 2
            p.dma("sync", qx[sl], DMA(xf[sl], xin3[:, :, tt:tt + TS]), writes=(xf_b[sl],))
            xo, xo_b = scr["t1"], scr["t1_buf"]
            for c in range(8):
                o = xo[:, c, :]
                p.op("vector", lambda e, o=o, x=xf[sl][:, c, :], b=SB2h[:, c:c + 1]:
                     e.tensor_scalar(o, x, b, None, op0=ALU.add),
                     reads=(xf_b[sl], SB2h_b), writes=(xo_b,))
                p.op("vector", lambda e, o=o, r=rv[:, c, tsi, :]:
                     e.scalar_tensor_tensor(o, r, 0.5, o, op0=ALU.mult, op1=ALU.add),
                     reads=(routed_b[c // 2],), writes=(xo_b,))
            p.dma("sync", qst[sl], DMA(xout3[:, :, tt:tt + TS], xo), reads=(xo_b,))




def emit_ret(p, C, ps, ctx, xin, xout, W, NT, cc_in, cc_out, n_cores, uid=""):
    nc = p.nc
    NRL = NT // NB
    sbuf = lambda n, s, d: ctx.enter_context(nc.sbuf_tensor(n + uid, s, d)).ap()

    wq = sbuf("ret_wq", [128, 8, 1024], BF16)
    wv = sbuf("ret_wv", [128, 8, 1024], BF16)
    wg = sbuf("ret_wg", [128, 8, 1024], BF16)
    wo = sbuf("ret_wo", [128, 8, 1024], BF16)
    w_b = Buf("ret_w")
    xfb = [sbuf(f"ret_xf{i}", [128, 8, NRL], F32) for i in range(2)]
    xfb_b = [Buf(f"ret_xf{i}") for i in range(2)]
    xlnb = sbuf("ret_xln", [128, 8, NRL], BF16)
    xlnb_b = Buf("ret_xln")
    vsb = sbuf("ret_vsb", [128, 8, NRL], F32)
    vsb_b = Buf("ret_vsb")
    ssb = sbuf("ret_ssb", [128, 8, NRL], F32)
    ssb_b = Buf("ret_ssb")
    ybf = sbuf("ret_ybf", [128, 8, NRL], BF16)
    ybf_b = Buf("ret_ybf")
    gbf = sbuf("ret_gbf", [128, 8, NRL], BF16)
    gbf_b = Buf("ret_gbf")
    E_sb = sbuf("ret_E", [128, 8, NB], F32)
    E_b = Buf("ret_E")
    Eg = sbuf("ret_Eg", [128, n_cores, 8, NB], F32)
    Eg_b = Buf("ret_Eg")
    Sini = sbuf("ret_S", [128, 8, NB], F32)
    Sini_b = Buf("ret_S")
    TABr = sbuf("ret_TABr", [128, 8, NRL], F32)
    TABr_b = Buf("ret_TABr")
    COEF = sbuf("ret_COEF", [128, 8, n_cores], F32)
    COEF_b = Buf("ret_COEF")
    LAMc = sbuf("ret_LAMc", [128, 8], F32)
    LAMc_b = Buf("ret_LAMc")
    GAMc = sbuf("ret_GAMc", [128, 8], F32)
    BETc = sbuf("ret_BETc", [128, 8], F32)
    GB_b = Buf("ret_GB")
    rowp = sbuf("ret_rowp", [1, 2048], F32)
    rowp_b = Buf("ret_rowp")
    lamr = sbuf("ret_lamr", [1, 1024], F32)
    lnlamr = sbuf("ret_lnlamr", [1, 1024], F32)
    lam_b = Buf("ret_lam")
    prod = sbuf("ret_prod", [128, NRL], F32)
    prod_b = Buf("ret_prod")
    scr = make_ln_scratch(p, ctx, min(NRL, 256), "ret" + uid)
    assert NRL <= 256

    qwr = p.q("qpg0", "gpsimd")
    qp = [p.q(f"qps{i}", "sync") for i in range(4)]
    qxA = [p.q(f"qx{i}", "sync") for i in range(2)]
    qstR = [p.q(f"qst{i}", "sync") for i in range(2)]
    qcc = p.q("qcc", "scalar")

    xin3 = xin.rearrange("(c p) t -> p c t", p=128)
    xout3 = xout.rearrange("(c p) t -> p c t", p=128)

    p.dma("gpsimd", qwr, DMA(wq, W["wqT"].rearrange("(kc p) m -> p kc m", p=128)), writes=(w_b,))
    p.dma("gpsimd", qwr, DMA(wv, W["wvT"].rearrange("(kc p) m -> p kc m", p=128)), writes=(w_b,))
    p.dma("gpsimd", qwr, DMA(wg, W["wgT"].rearrange("(kc p) m -> p kc m", p=128)), writes=(w_b,))
    p.dma("gpsimd", qwr, DMA(wo, W["woT"].rearrange("(kc p) m -> p kc m", p=128)), writes=(w_b,))
    p.dma("sync", qp[0], DMA(rowp[0:1, 0:16], W["beta"]), writes=(rowp_b,))
    p.op("scalar", lambda e: e.activation(rowp[0:1, 16:32], rowp[0:1, 0:16], AF.Exp,
                                          scale=-float(np.log(2.0))),
         reads=(rowp_b,), writes=(rowp_b,))
    p.op("vector", lambda e: e.tensor_scalar(rowp[0:1, 32:48], rowp[0:1, 16:32], -1.0, 1.0,
                                             op0=ALU.mult, op1=ALU.add),
         reads=(rowp_b,), writes=(rowp_b,))
    p.op("vector", lambda e: e.tensor_scalar_max(rowp[0:1, 32:48], rowp[0:1, 32:48], 1.1754944e-38),
         reads=(rowp_b,), writes=(rowp_b,))
    p.op("vector", lambda e: e.tensor_scalar_min(rowp[0:1, 32:48], rowp[0:1, 32:48], 1.0 - 1e-9),
         reads=(rowp_b,), writes=(rowp_b,))
    p.op("vector", lambda e: e.tensor_copy(
        lamr.rearrange("a (h d) -> a h d", h=16),
        rowp[0:1, 32:48].unsqueeze(2).to_broadcast([1, 16, 64])),
        reads=(rowp_b,), writes=(lam_b,))
    p.op("scalar", lambda e: e.activation(lnlamr, lamr, AF.Ln), reads=(lam_b,), writes=(lam_b,))
    row_to_cols(p, C, ps.f[5], ps.fb[5], lamr, lam_b, LAMc, LAMc_b, 8)
    p.dma("sync", qp[1], DMA(rowp[0:1, 1024:2048], W["lng"]), writes=(rowp_b,))
    row_to_cols(p, C, ps.f[5], ps.fb[5], rowp[0:1, 1024:2048], rowp_b, GAMc, GB_b, 8)
    p.dma("sync", qp[1], DMA(rowp[0:1, 1024:2048], W["lnb"]), writes=(rowp_b,))
    row_to_cols(p, C, ps.f[5], ps.fb[5], rowp[0:1, 1024:2048], rowp_b, BETc, GB_b, 8)
    p.dma("sync", qp[2], DMA(rowp[0:1, 48:48 + NRL], C.jrev[0:1, 256 - NRL:256]), writes=(rowp_b,))
    p.dma("sync", qp[3], DMA(rowp[0:1, 512:512 + n_cores], W["M"]), writes=(rowp_b,))
    for c in range(8):
        p.mm_group([MM(ps.f[6][:, 0:NRL], lnlamr[0:1, c * 128:(c + 1) * 128],
                       rowp[0:1, 48:48 + NRL], True, True)],
                   reads=(lam_b, rowp_b), writes=(ps.fb[6],))
        dst = TABr[:, c, :]
        p.op("scalar", lambda e, dst=dst: e.activation(dst, ps.f[6][:, 0:NRL], AF.Exp),
             reads=(ps.fb[6],), writes=(TABr_b,))
        p.mm_group([MM(ps.f[6][:, 256:256 + n_cores], lnlamr[0:1, c * 128:(c + 1) * 128],
                       rowp[0:1, 512:512 + n_cores], True, True)],
                   reads=(lam_b, rowp_b), writes=(ps.fb[6],))
        dst2 = COEF[:, c, :]
        p.op("scalar", lambda e, dst2=dst2: e.activation(dst2, ps.f[6][:, 256:256 + n_cores], AF.Exp),
             reads=(ps.fb[6],), writes=(COEF_b,))

    def proj(wt, dst3, dst_b, dtype_evac, bias_cols=None, b_buf=None, mul_tiles=None,
             mul_b=None, silu=False, addx=None, addx_b=None, out_f32=None):
        for bank in range(4):
            lfns = []
            for half in range(2):
                dm = 2 * bank + half
                colo = half * NRL
                for kc in range(8):
                    lfns.append(MM(ps.f[bank][:, colo:colo + NRL],
                                   wt[:, kc, dm * 128:(dm + 1) * 128], xlnb[:, kc, :],
                                   half == 0 and kc == 0, half == 1 and kc == 7))
            p.mm_group(lfns, reads=(w_b, xlnb_b), writes=(ps.fb[bank],))
        for dm in range(8):
            bank, colo = dm // 2, (dm % 2) * NRL
            src = ps.f[bank][:, colo:colo + NRL]
            dst = dst3[:, dm, :]
            if silu:
                p.op("scalar", lambda e, dst=dst, src=src: e.activation(dst, src, AF.Silu),
                     reads=(ps.fb[bank],), writes=(dst_b,))
            elif mul_tiles is not None:
                m = mul_tiles[:, dm, :]
                p.op("vector", lambda e, dst=dst, src=src, m=m: e.tensor_mul(dst, src, m),
                     reads=(ps.fb[bank], mul_b), writes=(dst_b,))
            elif addx is not None:
                a = addx[:, dm, :]
                p.op("vector", lambda e, dst=dst, src=src, a=a: e.tensor_add(dst, a, src),
                     reads=(ps.fb[bank], addx_b), writes=(dst_b,))
            else:
                p.op("vector", lambda e, dst=dst, src=src: e.tensor_copy(dst, src),
                     reads=(ps.fb[bank],), writes=(dst_b,))

    for b in range(NB):
        sl = b % 2
        t0 = b * NRL
        p.dma("sync", qxA[sl], DMA(xfb[sl], xin3[:, :, t0:t0 + NRL]), writes=(xfb_b[sl],))
        emit_ln(p, C, ps, xfb[sl], xfb_b[sl], xlnb, xlnb_b, scr, NRL)
        proj(wv, vsb, vsb_b, F32)
        for c in range(8):
            pr = prod[:, :]
            p.op("vector", lambda e, c=c: e.tensor_mul(prod, vsb[:, c, :], TABr[:, c, :]),
                 reads=(vsb_b, TABr_b), writes=(prod_b,))
            dst = E_sb[:, c, b:b + 1]
            p.op("vector", lambda e, dst=dst: e.reduce_sum(dst, prod, axis=AX.X),
                 reads=(prod_b,), writes=(E_b,))
    ccin_b = Buf("cc_in")
    cco_b = Buf("cc_out")
    p.dma("scalar", qcc, DMA(cc_in.rearrange("(c p) b -> p c b", p=128), E_sb),
          reads=(E_b,), writes=(ccin_b,))
    p.op("gpsimd", lambda e: e.collective_compute(
        "AllGather", ALU.bypass, ins=[cc_in.opt()], outs=[cc_out.opt()],
        replica_groups=[list(range(n_cores))]),
        reads=(ccin_b,), writes=(cco_b,))
    p.dma("scalar", qcc, DMA(Eg, cc_out.rearrange("(r c p) b -> p r c b", p=128, c=8)),
          reads=(cco_b,), writes=(Eg_b,))
    p.op("vector", lambda e: e.memset(Sini.rearrange("p c b -> p (c b)"), 0.0), writes=(Sini_b,))
    for c in range(8):
        for j in range(n_cores):
            dst = Sini[:, c, :]
            src = Eg[:, j, c, :]
            cf = COEF[:, c, j:j + 1]
            p.op("vector", lambda e, dst=dst, src=src, cf=cf: e.scalar_tensor_tensor(
                dst, src, cf, dst, op0=ALU.mult, op1=ALU.add),
                reads=(Eg_b, COEF_b), writes=(Sini_b,))
    for b in range(NB):
        sl = b % 2
        t0 = b * NRL
        p.dma("sync", qxA[sl], DMA(xfb[sl], xin3[:, :, t0:t0 + NRL]), writes=(xfb_b[sl],))
        emit_ln(p, C, ps, xfb[sl], xfb_b[sl], xlnb, xlnb_b, scr, NRL)
        proj(wv, vsb, vsb_b, F32)
        for c in range(8):
            dst = vsb[:, c, 0:1]
            sc_ = Sini[:, c, b:b + 1]
            lc = LAMc[:, c:c + 1]
            p.op("vector", lambda e, dst=dst, sc_=sc_, lc=lc: e.scalar_tensor_tensor(
                dst, sc_, lc, dst, op0=ALU.mult, op1=ALU.add),
                reads=(Sini_b, LAMc_b), writes=(vsb_b,))
        for c in range(8):
            dst = ssb[:, c, :]
            src = vsb[:, c, :]
            lb = LAMc[:, c:c + 1].to_broadcast([128, NRL])
            p.op("vector", lambda e, dst=dst, src=src, lb=lb: e.tensor_tensor_scan(
                out=dst, data0=lb, data1=src, initial=0.0, op0=ALU.mult, op1=ALU.add),
                reads=(vsb_b, LAMc_b), writes=(ssb_b,))
        proj(wq, vsb, vsb_b, F32, mul_tiles=ssb, mul_b=ssb_b)
        emit_ln(p, C, ps, vsb, vsb_b, ybf, ybf_b, scr, NRL, gam_col=GAMc, bet_col=BETc,
                colbuf=GB_b)
        proj(wg, gbf, gbf_b, BF16, silu=True)
        yv = ybf.rearrange("p c t -> p (c t)")
        gv = gbf.rearrange("p c t -> p (c t)")
        p.op("vector", lambda e, yv=yv, gv=gv: e.tensor_mul(yv, yv, gv),
             reads=(gbf_b,), writes=(ybf_b,))
        for bank in range(4):
            lfns = []
            for half in range(2):
                dm = 2 * bank + half
                colo = half * NRL
                for kc in range(8):
                    lfns.append(MM(ps.f[bank][:, colo:colo + NRL],
                                   wo[:, kc, dm * 128:(dm + 1) * 128], ybf[:, kc, :],
                                   half == 0 and kc == 0, half == 1 and kc == 7))
            p.mm_group(lfns, reads=(w_b, ybf_b), writes=(ps.fb[bank],))
        for dm in range(8):
            bank, colo = dm // 2, (dm % 2) * NRL
            src = ps.f[bank][:, colo:colo + NRL]
            dst = ssb[:, dm, :]
            a = xfb[sl][:, dm, :]
            p.op("vector", lambda e, dst=dst, src=src, a=a: e.tensor_add(dst, a, src),
                 reads=(ps.fb[bank], xfb_b[sl]), writes=(ssb_b,))
        p.dma("sync", qstR[sl], DMA(xout3[:, :, t0:t0 + NRL], ssb), reads=(ssb_b,))




def emit_ffn1(p, C, ps, ctx, xin, xout, W, NT, uid=""):
    nc = p.nc
    TS = 256
    NSUB = NT // TS
    sbuf = lambda n, s, d: ctx.enter_context(nc.sbuf_tensor(n + uid, s, d)).ap()

    import contextlib
    bw1 = sbuf("f1_bw1", [128, 8, 4096], BF16)
    bw1_b = Buf("bw1")
    bw2 = sbuf("f1_bw2", [128, 32, 1024], BF16)
    bw2_b = Buf("bw2")
    B1c = sbuf("f1_B1c", [128, 32], F32)
    B2c = sbuf("f1_B2c", [128, 8], F32)
    Bc_b = Buf("f1Bc")
    pctx = ctx.enter_context(contextlib.ExitStack())
    psbuf = lambda n, s, d: pctx.enter_context(nc.sbuf_tensor(n + uid, s, d)).ap()
    wf = [psbuf(f"f1_wf{i}", [128, 4096], F32) for i in range(2)]
    wf_b = [Buf(f"wf{i}") for i in range(2)]
    sgn = psbuf("f1_sgn", [128, 4096], BF16)
    sgn_b = Buf("sgn")
    alb = psbuf("f1_alb", [128, 4096], BF16)
    alb_b = Buf("alb")
    alr = psbuf("f1_alr", [1, 4096], F32)
    alr_b = Buf("alr")

    qwa = [p.q(f"qx{i}", "sync") for i in range(2)]
    qxl = [p.q(f"qw{i}", "gpsimd") for i in range(2)]
    qst = [p.q(f"qst{i}", "sync") for i in range(2)]
    qp = [p.q(f"qps{i}", "sync") for i in range(4)]

    xin3 = xin.rearrange("(c p) t -> p c t", p=128)
    xout3 = xout.rearrange("(c p) t -> p c t", p=128)

    p.op("vector", lambda e: e.memset(alr, 0.0), writes=(alr_b,))
    for dc in range(8):
        sl = dc % 2
        p.dma("sync", qwa[sl], DMA(wf[sl], W["w1T"][dc * 128:(dc + 1) * 128, :]), writes=(wf_b[sl],))
        p.op("scalar", lambda e, sl=sl: e.activation(sgn, wf[sl], AF.Abs),
             reads=(wf_b[sl],), writes=(sgn_b,))
        for j in range(8):
            p.mm_group([MM(ps.f[5][:, 0:512][0:1, :], C.onecb, sgn[:, j * 512:(j + 1) * 512],
                           True, True)], reads=(sgn_b, C.buf), writes=(ps.fb[5],))
            dst = alr[0:1, j * 512:(j + 1) * 512]
            p.op("vector", lambda e, dst=dst: e.tensor_add(dst, dst, ps.f[5][0:1, 0:512]),
                 reads=(ps.fb[5],), writes=(alr_b,))
    p.op("vector", lambda e: e.tensor_scalar(alr, alr, 1.0 / D, None, op0=ALU.mult),
         reads=(alr_b,), writes=(alr_b,))
    for j in range(8):
        p.mm_group([MM(ps.f[5][:, 0:512], C.onerf, alr[0:1, j * 512:(j + 1) * 512], True, True)],
                   reads=(alr_b, C.buf), writes=(ps.fb[5],))
        dst = alb[:, j * 512:(j + 1) * 512]
        p.op("scalar", lambda e, dst=dst: e.activation(dst, ps.f[5][:, 0:512], AF.Identity),
             reads=(ps.fb[5],), writes=(alb_b,))
    for dc in range(8):
        sl = dc % 2
        p.dma("sync", qwa[sl], DMA(wf[sl], W["w1T"][dc * 128:(dc + 1) * 128, :]), writes=(wf_b[sl],))
        p.op("scalar", lambda e, sl=sl: e.activation(sgn, wf[sl], AF.Sign),
             reads=(wf_b[sl],), writes=(sgn_b,))
        dst = bw1[:, dc, :]
        p.op("vector", lambda e, dst=dst: e.tensor_mul(dst, sgn, alb),
             reads=(sgn_b, alb_b), writes=(bw1_b,))
    p.op("vector", lambda e: e.memset(alr[0:1, 0:1024], 0.0), writes=(alr_b,))
    for hc in range(32):
        sl = hc % 2
        p.dma("sync", qwa[sl], DMA(wf[sl][:, 0:1024], W["w2T"][hc * 128:(hc + 1) * 128, :]),
              writes=(wf_b[sl],))
        p.op("scalar", lambda e, sl=sl: e.activation(sgn[:, 0:1024], wf[sl][:, 0:1024], AF.Abs),
             reads=(wf_b[sl],), writes=(sgn_b,))
        for j in range(2):
            p.mm_group([MM(ps.f[5][0:1, 0:512], C.onecb, sgn[:, j * 512:(j + 1) * 512], True, True)],
                       reads=(sgn_b, C.buf), writes=(ps.fb[5],))
            dst = alr[0:1, j * 512:(j + 1) * 512]
            p.op("vector", lambda e, dst=dst: e.tensor_add(dst, dst, ps.f[5][0:1, 0:512]),
                 reads=(ps.fb[5],), writes=(alr_b,))
    p.op("vector", lambda e: e.tensor_scalar(alr[0:1, 0:1024], alr[0:1, 0:1024], 1.0 / H1, None,
                                             op0=ALU.mult), reads=(alr_b,), writes=(alr_b,))
    for j in range(2):
        p.mm_group([MM(ps.f[5][:, 0:512], C.onerf, alr[0:1, j * 512:(j + 1) * 512], True, True)],
                   reads=(alr_b, C.buf), writes=(ps.fb[5],))
        dst = alb[:, j * 512:(j + 1) * 512]
        p.op("scalar", lambda e, dst=dst: e.activation(dst, ps.f[5][:, 0:512], AF.Identity),
             reads=(ps.fb[5],), writes=(alb_b,))
    for hc in range(32):
        sl = hc % 2
        p.dma("sync", qwa[sl], DMA(wf[sl][:, 0:1024], W["w2T"][hc * 128:(hc + 1) * 128, :]),
              writes=(wf_b[sl],))
        p.op("scalar", lambda e, sl=sl: e.activation(sgn[:, 0:1024], wf[sl][:, 0:1024], AF.Sign),
             reads=(wf_b[sl],), writes=(sgn_b,))
        dst = bw2[:, hc, :]
        p.op("vector", lambda e, dst=dst: e.tensor_mul(dst, sgn[:, 0:1024], alb[:, 0:1024]),
             reads=(sgn_b, alb_b), writes=(bw2_b,))
    p.dma("sync", qp[0], DMA(alr, W["b1"]), writes=(alr_b,))
    row_to_cols(p, C, ps.f[5], ps.fb[5], alr, alr_b, B1c, Bc_b, 32)
    p.dma("sync", qp[0], DMA(alr[0:1, 0:1024], W["b2"]), writes=(alr_b,))
    row_to_cols(p, C, ps.f[5], ps.fb[5], alr[0:1, 0:1024], alr_b, B2c, Bc_b, 8, scale=0.5)

    pctx.close()
    p.barrier()
    xf = [sbuf(f"f1_xf{i}", [128, 8, TS], F32) for i in range(2)]
    xf_b = [Buf(f"f1xf{i}") for i in range(2)]
    xo = [sbuf(f"f1_xo{i}", [128, 8, TS], F32) for i in range(2)]
    xo_b = [Buf(f"f1xo{i}") for i in range(2)]
    xln = sbuf("f1_xln", [128, 8, TS], BF16)
    xln_b = Buf("f1xln")
    hbf = sbuf("f1_hbf", [128, 32, TS], BF16)
    hbf_b = Buf("f1hbf")
    scr = make_ln_scratch(p, ctx, TS, "f1" + uid)

    for tsi in range(NSUB):
        tt = tsi * TS
        sl = tsi % 2
        p.dma("gpsimd", qxl[sl], DMA(xf[sl], xin3[:, :, tt:tt + TS]), writes=(xf_b[sl],))
        emit_ln(p, C, ps, xf[sl], xf_b[sl], xln, xln_b, scr, TS)
        for hm in range(32):
            bank = 4 + hm % 2
            lfns = [MM(ps.f[bank][:, 0:TS], bw1[:, kc, hm * 128:(hm + 1) * 128],
                       xln[:, kc, :], kc == 0, kc == 7) for kc in range(8)]
            p.mm_group(lfns, reads=(bw1_b, xln_b), writes=(ps.fb[bank],))
            dst = hbf[:, hm, :]
            bc = B1c[:, hm:hm + 1]
            p.op("scalar", lambda e, dst=dst, bc=bc, bank=bank: e.activation(
                dst, ps.f[bank][:, 0:TS], AF.Gelu, bias=bc),
                reads=(ps.fb[bank], Bc_b), writes=(hbf_b,))
        for dm in range(8):
            bank, colo = dm % 4, 0
            lfns = [MM(ps.f[bank][:, colo:colo + TS], bw2[:, kh, dm * 128:(dm + 1) * 128],
                       hbf[:, kh, :], kh == 0, kh == 31) for kh in range(32)]
            p.mm_group(lfns, reads=(bw2_b, hbf_b), writes=(ps.fb[bank],))
            o = xo[sl][:, dm, :]
            bc = B2c[:, dm:dm + 1]
            src = ps.f[bank][:, colo:colo + TS]
            p.op("scalar", lambda e, o=o, bc=bc, src=src: e.activation(
                o, src, AF.Identity, bias=bc, scale=0.5),
                reads=(ps.fb[bank], Bc_b), writes=(xo_b[sl],))
            a = xf[sl][:, dm, :]
            p.op("vector", lambda e, o=o, a=a: e.tensor_add(o, o, a),
                 reads=(xf_b[sl],), writes=(xo_b[sl],))
        p.dma("sync", qst[sl], DMA(xout3[:, :, tt:tt + TS], xo[sl]), reads=(xo_b[sl],))




def emit_attn(p, C, ps, ctx, xin, xout, W, NT, uid=""):
    nc = p.nc
    NRL = NT // NB
    RB = min(16, NRL)
    NBLK = NRL // RB
    TT = NB * RB
    sbuf = lambda n, s, d: ctx.enter_context(nc.sbuf_tensor(n + uid, s, d)).ap()

    wq = sbuf("at_wq", [128, 8, 1024], BF16)
    wk = sbuf("at_wk", [128, 8, 1024], BF16)
    wv = sbuf("at_wv", [128, 8, 1024], BF16)
    w_b = Buf("at_w")
    BQc = sbuf("at_BQc", [128, 8], F32)
    BKc = sbuf("at_BKc", [128, 8], F32)
    BVc = sbuf("at_BVc", [128, 8], F32)
    Bc_b = Buf("at_Bc")
    rowp = sbuf("at_rowp", [1, 1024], F32)
    rowp_b = Buf("at_rowp")
    xfa = [sbuf(f"at_xf{i}", [128, 8, NB, RB], F32) for i in range(2)]
    xfa_b = [Buf(f"at_xf{i}") for i in range(2)]
    xln = sbuf("at_xln", [128, 8, TT], BF16)
    xln_b = Buf("at_xln")
    qh = sbuf("at_qh", [64, 16, TT], BF16)
    kh = sbuf("at_kh", [64, 16, TT], BF16)
    qkv_b = Buf("at_qkv")
    vsb = sbuf("at_v", [128, 8, TT], BF16)
    qcont = sbuf("at_qc", [64, 16, NB], BF16)
    kcont = sbuf("at_kc", [64, 16, NB], BF16)
    qc_b = Buf("at_qc")
    vT = sbuf("at_vT", [32, 8, 128], BF16)
    vT_b = Buf("at_vT")
    scf = sbuf("at_scf", [32, 512], F32)
    exf = sbuf("at_exf", [32, 512], F32)
    sm_b = Buf("at_sm")
    m16 = sbuf("at_m16", [32, 16 * 3], F32)
    m16_b = Buf("at_m16")
    abf = sbuf("at_abf", [32, 512], BF16)
    abf_b = Buf("at_abf")
    aT = sbuf("at_aT", [32, 512], BF16)
    aT_b = Buf("at_aT")
    ctxf = sbuf("at_ctxf", [128, 8, RB, NB], F32)
    odd = sbuf("at_odd", [64, 8, RB, NB], F32)
    ctxf_b = Buf("at_ctxf")
    odd_b = Buf("at_odd")
    scr = make_ln_scratch(p, ctx, 256, "at" + uid)

    qwr = p.q("qpg0", "gpsimd")
    qp = [p.q(f"qps{i}", "sync") for i in range(4)]
    qxA = [p.q(f"qx{i}", "sync") for i in range(2)]
    qstA = [p.q(f"qst{i}", "sync") for i in range(2)]
    qsh = p.q("qshift", "scalar")

    xin4 = xin.rearrange("(c p) (a r) -> p c a r", p=128, a=NB)
    xout4 = xout.rearrange("(c p) (a r) -> p c a r", p=128, a=NB)

    p.dma("gpsimd", qwr, DMA(wq, W["wqT"].rearrange("(kc p) m -> p kc m", p=128)), writes=(w_b,))
    p.dma("gpsimd", qwr, DMA(wk, W["wkT"].rearrange("(kc p) m -> p kc m", p=128)), writes=(w_b,))
    p.dma("gpsimd", qwr, DMA(wv, W["wvT"].rearrange("(kc p) m -> p kc m", p=128)), writes=(w_b,))
    p.dma("sync", qp[0], DMA(rowp, W["bq"]), writes=(rowp_b,))
    row_to_cols(p, C, ps.f[5], ps.fb[5], rowp, rowp_b, BQc, Bc_b, 8, scale=0.125)
    p.dma("sync", qp[0], DMA(rowp, W["bk"]), writes=(rowp_b,))
    row_to_cols(p, C, ps.f[5], ps.fb[5], rowp, rowp_b, BKc, Bc_b, 8)
    p.dma("sync", qp[0], DMA(rowp, W["bv"]), writes=(rowp_b,))
    row_to_cols(p, C, ps.f[5], ps.fb[5], rowp, rowp_b, BVc, Bc_b, 8)
    p.op("vector", lambda e: e.memset(ctxf.rearrange("p c r a -> p (c r a)"), 0.0),
         writes=(ctxf_b,))

    for blk in range(NBLK):
        r0 = blk * RB
        sl = blk % 2
        for c in range(8):
            p.dma("sync", qxA[sl], DMA(xfa[sl][:, c, :, :], xin4[:, c, :, r0:r0 + RB]),
                  writes=(xfa_b[sl],))
        xfv = xfa[sl].rearrange("p c a r -> p c (a r)")
        for j in range(TT // 256):
            xfs = xfv.rearrange("p c (n t) -> p c n t", t=256)[:, :, j, :]
            xls = xln.rearrange("p c (n t) -> p c n t", t=256)[:, :, j, :]
            emit_ln(p, C, ps, xfs, xfa_b[sl], xls, xln_b, scr, 256)
        for (wt, dst, bc, scale) in ((wq, qh, BQc, 0.125), (wk, kh, BKc, 1.0)):
            for h in range(16):
                bank = 4 + h % 2
                lfns = [MM(ps.f[bank][0:64, 0:TT], wt[:, kc, h * 64:(h + 1) * 64],
                           xln[:, kc, :], kc == 0, kc == 7) for kc in range(8)]
                p.mm_group(lfns, reads=(w_b, xln_b), writes=(ps.fb[bank],))
                o = dst[0:64, h, :]
                b = bc[(h % 2) * 64:(h % 2) * 64 + 64, h // 2: h // 2 + 1]
                p.op("scalar", lambda e, o=o, b=b, bank=bank, scale=scale: e.activation(
                    o, ps.f[bank][0:64, 0:TT], AF.Identity, bias=b, scale=scale),
                    reads=(ps.fb[bank], Bc_b), writes=(qkv_b,))
        for dm in range(8):
            bank = 4 + dm % 2
            lfns = [MM(ps.f[bank][:, 0:TT], wv[:, kc, dm * 128:(dm + 1) * 128],
                       xln[:, kc, :], kc == 0, kc == 7) for kc in range(8)]
            p.mm_group(lfns, reads=(w_b, xln_b), writes=(ps.fb[bank],))
            o = vsb[:, dm, :]
            b = BVc[:, dm:dm + 1]
            p.op("scalar", lambda e, o=o, b=b, bank=bank: e.activation(
                o, ps.f[bank][:, 0:TT], AF.Identity, bias=b),
                reads=(ps.fb[bank], Bc_b), writes=(qkv_b,))
        qh4 = qh.rearrange("p h (a r) -> p h a r", a=NB)
        kh4 = kh.rearrange("p h (a r) -> p h a r", a=NB)
        v4 = vsb.rearrange("p c (a r) -> p c a r", a=NB)
        for rr in range(RB):
            p.op("vector", lambda e, rr=rr: e.tensor_copy(qcont, qh4[0:64, :, :, rr]),
                 reads=(qkv_b,), writes=(qc_b,))
            p.op("vector", lambda e, rr=rr: e.tensor_copy(kcont, kh4[0:64, :, :, rr]),
                 reads=(qkv_b,), writes=(qc_b,))
            lfns = [TRG(ps.b[0:32, c * 128:(c + 1) * 128], v4[:, c, :, rr], C.idb,
                        c == 0, c == 7) for c in range(8)]
            p.mm_group(lfns, reads=(qkv_b, C.buf), writes=(ps.bb,), inc_each=True)
            p.op("vector", lambda e: e.tensor_copy(vT.rearrange("a c m -> a (c m)"),
                                                   ps.b[0:32, 0:1024]),
                 reads=(ps.bb,), writes=(vT_b,))
            sbank = 4 + rr % 2
            lfns = [MM(ps.f[sbank][0:32, h * 32:(h + 1) * 32],
                       qcont[0:64, h, :], kcont[0:64, h, :],
                       h == 0, h == 15) for h in range(16)]
            p.mm_group(lfns, reads=(qc_b,), writes=(ps.fb[sbank],), inc_each=True)
            m3 = scf.rearrange("a (h j) -> a h j", j=32)
            e3 = exf.rearrange("a (h j) -> a h j", j=32)
            p.op("vector", lambda e, sbank=sbank: e.tensor_add(
                m3, ps.f[sbank][0:32, 0:512].rearrange("a (h j) -> a h j", j=32),
                C.caus.unsqueeze(1).to_broadcast([32, 16, 32])),
                reads=(ps.fb[sbank], C.buf), writes=(sm_b,))
            p.op("vector", lambda e: e.reduce_max(m16[:, 0:16], m3, axis=AX.X, negate=True),
                 reads=(sm_b,), writes=(m16_b,))
            p.op("vector", lambda e: e.tensor_add(
                e3, m3, m16[:, 0:16].unsqueeze(2).to_broadcast([32, 16, 32])),
                reads=(sm_b, m16_b), writes=(sm_b,))
            p.op("scalar", lambda e: e.activation(exf, exf, AF.Exp), reads=(sm_b,), writes=(sm_b,))
            p.op("vector", lambda e: e.reduce_sum(m16[:, 16:32], e3, axis=AX.X),
                 reads=(sm_b,), writes=(m16_b,))
            p.op("vector", lambda e: e.tensor_scalar(m16[:, 16:32], m16[:, 16:32], 1.0, None,
                                                     op0=ALU.add), reads=(m16_b,), writes=(m16_b,))
            p.op("vector", lambda e: e.reciprocal(m16[:, 32:48], m16[:, 16:32]),
                 reads=(m16_b,), writes=(m16_b,))
            p.op("vector", lambda e: e.tensor_mul(
                abf.rearrange("a (h j) -> a h j", j=32), e3,
                m16[:, 32:48].unsqueeze(2).to_broadcast([32, 16, 32])),
                reads=(sm_b, m16_b), writes=(abf_b,))
            lfns = [TRG(ps.b[0:32, 512 + h * 32: 512 + (h + 1) * 32],
                        abf[0:32, h * 32:(h + 1) * 32], C.idb[0:32, 0:32],
                        h == 0, h == 15) for h in range(16)]
            p.mm_group(lfns, reads=(abf_b, C.buf), writes=(ps.bb,), inc_each=True)
            p.op("vector", lambda e: e.tensor_copy(aT, ps.b[0:32, 512:1024]),
                 reads=(ps.bb,), writes=(aT_b,))
            cbank = rr % 2
            lfns = [MM(ps.f[cbank][0:64, h * 32:(h + 1) * 32],
                       vT[0:32, h // 2, (h % 2) * 64:(h % 2) * 64 + 64],
                       aT[0:32, h * 32:(h + 1) * 32],
                       h == 0, h == 15) for h in range(16)]
            p.mm_group(lfns, reads=(vT_b, aT_b), writes=(ps.fb[cbank],), inc_each=True)
            pv = ps.f[cbank][0:64, 0:512].rearrange("p (c two a) -> p c two a", two=2, a=32)
            p.op("vector", lambda e, pv=pv, rr=rr: e.tensor_copy(
                ctxf[0:64, :, rr, :], pv[:, :, 0, :]),
                reads=(ps.fb[cbank],), writes=(ctxf_b,))
            p.op("vector", lambda e, pv=pv, rr=rr: e.tensor_copy(
                odd[0:64, :, rr, :], pv[:, :, 1, :]),
                reads=(ps.fb[cbank],), writes=(odd_b,))
        p.dma("scalar", qsh, DMA(ctxf[64:128, :, :, :], odd[0:64, :, :, :]),
              reads=(odd_b,), writes=(ctxf_b,))
        for c in range(8):
            for a in range(NB):
                o = xfa[sl][:, c, a, :]
                i1 = ctxf[:, c, :, a]
                p.op("vector", lambda e, o=o, i1=i1: e.tensor_add(o, o, i1),
                     reads=(ctxf_b,), writes=(xfa_b[sl],))
        for c in range(8):
            p.dma("sync", qstA[sl], DMA(xout4[:, c, :, r0:r0 + RB], xfa[sl][:, c, :, :]),
                  reads=(xfa_b[sl],))




PHASES = ("moe1", "ret", "ffn1a", "attn", "ffn1b", "moe2")


def build_program(NT=8192, phases=("moe", "ret", "moe", "ffn1", "attn", "ffn1"), n_cores=8):
    nc = bass.Bass("TRN2", target_bir_lowering=False, num_devices=n_cores)
    x_in = nc.dram_tensor("x", [D, NT], F32, kind="ExternalInput").ap()
    y = nc.dram_tensor("y", [D, NT], F32, kind="ExternalOutput").ap()
    xr = nc.dram_tensor("xr", [D, NT], F32, kind="Internal").ap()
    cc_in = nc.dram_tensor("cc_in", [D, NB], F32, kind="Internal").ap()
    cc_out = nc.dram_tensor("cc_out", [n_cores * D, NB], F32, kind="Internal").ap()

    WM = {k: nc.dram_tensor("m_" + k, shp, F32, kind="ExternalInput").ap() for k, shp in [
        ("rw", [D, NE]), ("rb", [1, NE]), ("w1", [NE, D, EH]), ("b1", [NE, EH]),
        ("w2", [NE, EH, D]), ("b2", [NE, D]), ("sw1", [ES, D, HS]), ("sb1", [ES, HS]),
        ("sw2", [ES, HS, D]), ("sb2", [ES, D])]}
    WR = {k: nc.dram_tensor("r_" + k, shp, F32, kind="ExternalInput").ap() for k, shp in [
        ("wqT", [D, D]), ("wvT", [D, D]), ("wgT", [D, D]), ("woT", [D, D]),
        ("beta", [1, H]), ("lng", [1, D]), ("lnb", [1, D]), ("M", [1, n_cores])]}
    WA = {k: nc.dram_tensor("a_" + k, shp, F32, kind="ExternalInput").ap() for k, shp in [
        ("wqT", [D, D]), ("wkT", [D, D]), ("wvT", [D, D]),
        ("bq", [1, D]), ("bk", [1, D]), ("bv", [1, D])]}
    WF = {k: nc.dram_tensor("f_" + k, shp, F32, kind="ExternalInput").ap() for k, shp in [
        ("w1T", [D, H1]), ("w2T", [H1, D]), ("b1", [1, H1]), ("b2", [1, D])]}

    p = P(nc)
    C = Consts(p)
    ps = Psum(p)

    import contextlib
    cur = x_in
    for i, ph in enumerate(phases):
        dst = y if i == len(phases) - 1 else xr
        with contextlib.ExitStack() as ctx:
            if ph == "moe":
                emit_moe(p, C, ps, ctx, cur, dst, WM, NT, uid=str(i))
            elif ph == "ret":
                emit_ret(p, C, ps, ctx, cur, dst, WR, NT, cc_in, cc_out, n_cores, uid=str(i))
            elif ph == "ffn1":
                emit_ffn1(p, C, ps, ctx, cur, dst, WF, NT, uid=str(i))
            elif ph == "attn":
                emit_attn(p, C, ps, ctx, cur, dst, WA, NT, uid=str(i))
            else:
                raise ValueError(ph)
            p.barrier()
        cur = dst
    p.build()
    return nc


def make_in_maps(inputs, NT=8192, n_cores=8):
    f32 = np.float32
    ca = np.ascontiguousarray
    x = np.asarray(inputs["L0_attn_x"], f32)
    NRL = NT // NB
    shared = {
        "m_rw": ca(np.asarray(inputs["L0_ffn_router_w"], f32)),
        "m_rb": ca(np.asarray(inputs["L0_ffn_router_b"], f32)[None, :]),
        "m_w1": ca(np.asarray(inputs["L0_ffn_w1"], f32)),
        "m_b1": ca(np.asarray(inputs["L0_ffn_b1"], f32)),
        "m_w2": ca(np.asarray(inputs["L0_ffn_w2"], f32)),
        "m_b2": ca(np.asarray(inputs["L0_ffn_b2"], f32)),
        "m_sw1": ca(np.asarray(inputs["L0_ffn_sw1"], f32)),
        "m_sb1": ca(np.asarray(inputs["L0_ffn_sb1"], f32)),
        "m_sw2": ca(np.asarray(inputs["L0_ffn_sw2"], f32)),
        "m_sb2": ca(np.asarray(inputs["L0_ffn_sb2"], f32)),
        "r_wqT": ca(np.asarray(inputs["L0_attn_Wq"], f32).T),
        "r_wvT": ca(np.asarray(inputs["L0_attn_Wv"], f32).T),
        "r_wgT": ca(np.asarray(inputs["L0_attn_Wg"], f32).T),
        "r_woT": ca(np.asarray(inputs["L0_attn_Wo"], f32).T),
        "r_beta": ca(np.asarray(inputs["L0_attn_beta"], f32)[None, :]),
        "r_lng": ca(np.asarray(inputs["L0_attn_ln_gamma"], f32)[None, :]),
        "r_lnb": ca(np.asarray(inputs["L0_attn_ln_beta"], f32)[None, :]),
        "a_wqT": ca(np.asarray(inputs["L1_attn_Wq"], f32).T),
        "a_wkT": ca(np.asarray(inputs["L1_attn_Wk"], f32).T),
        "a_wvT": ca(np.asarray(inputs["L1_attn_Wv"], f32).T),
        "a_bq": ca(np.asarray(inputs["L1_attn_bq"], f32)[None, :]),
        "a_bk": ca(np.asarray(inputs["L1_attn_bk"], f32)[None, :]),
        "a_bv": ca(np.asarray(inputs["L1_attn_bv"], f32)[None, :]),
        "f_w1T": ca(np.asarray(inputs["L1_ffn_w1"], f32).T),
        "f_w2T": ca(np.asarray(inputs["L1_ffn_w2"], f32).T),
        "f_b1": ca(np.asarray(inputs["L1_ffn_b1"], f32)[None, :]),
        "f_b2": ca(np.asarray(inputs["L1_ffn_b2"], f32)[None, :]),
    }
    in_maps = []
    for c in range(n_cores):
        xc = x[:, c * NRL:(c + 1) * NRL, :].reshape(NB * NRL, D)
        m = dict(shared)
        m["x"] = ca(xc.T)
        M = np.full((1, n_cores), 1e13, f32)
        for j in range(c):
            M[0, j] = float(NRL) * (c - 1 - j)
        m["r_M"] = M
        in_maps.append(m)
    return in_maps


def kernel(**inputs):
    NT = 8192
    n_cores = 8
    nc = build_program(NT=NT, phases=("moe", "ret", "moe", "ffn1", "attn", "ffn1"),
                       n_cores=n_cores)
    in_maps = make_in_maps(inputs, NT, n_cores)
    res = run_bass_kernel_spmd(nc, in_maps, core_ids=list(range(n_cores)))
    NRL = NT // NB
    out = np.empty((NB, NC_ * NRL, D), np.float32)
    for c in range(n_cores):
        out[:, c * NRL:(c + 1) * NRL, :] = res.results[c]["y"].T.reshape(NB, NRL, D)
    return out
